# version 01 (final)
# baseline (speedup 1.0000x reference)
import numpy as np
import concourse.bass as bass
import concourse.mybir as mybir
from concourse.bass_utils import run_bass_kernel_spmd

F32 = mybir.dt.float32
F32R = mybir.dt.float32r
BF16 = mybir.dt.bfloat16
I32 = mybir.dt.int32
U8 = mybir.dt.uint8
AF = mybir.ActivationFunctionType
ALU = mybir.AluOpType
AX = mybir.AxisListType

NCORE = 8
T = 8192
TC = 1024
D = 2048
H = 32
P = 64
N = 128
KCONV = 4
PROJ = 12320
Q = 128
NCH = 8
BLK = 512
NBLK = 2
DI = 8192
MAGIC = 0x5F3759DF

ENGS = ("sync", "tensor", "vector", "scalar", "gpsimd")


class Prog:

    def __init__(self, nc):
        self.nc = nc
        self.streams = {e: [] for e in ENGS}
        self.sems = {}
        self.ctr = {}

    def sem(self, name):
        self.sems[name] = self.nc.semaphore(name).__enter__()
        self.ctr[name] = 0

    def emit(self, eng, fn):
        self.streams[eng].append(fn)

    def val(self, name):
        return self.ctr[name]

    def op(self, eng, method, *a, sem=None, **kw):
        tgt = None
        if sem:
            self.ctr[sem] += 1
            tgt = self.ctr[sem]
        s = self.sems[sem] if sem else None

        def f(e, method=method, a=a, kw=kw, s=s):
            inst = getattr(e, method)(*a, **kw)
            if s is not None:
                inst.then_inc(s, 1)
        self.emit(eng, f)
        return tgt

    def dma(self, eng, out, in_, sem):
        self.wait(eng, sem)
        self.ctr[sem] += 16

        def f(e, out=out, in_=in_, s=self.sems[sem]):
            e.dma_start(out=out, in_=in_).then_inc(s, 16)
        self.emit(eng, f)
        return self.ctr[sem]

    def dump_barrier(self):
        for e in ("tensor", "vector", "scalar", "gpsimd"):
            self.wait(e, "st")

    def wait(self, eng, sem, tgt=None):
        if tgt is None:
            tgt = self.ctr[sem]
        assert tgt <= self.ctr[sem], (sem, tgt, self.ctr[sem])
        if tgt <= 0:
            return

        def f(e, s=self.sems[sem], tgt=tgt):
            e.wait_ge(s, tgt)
        self.emit(eng, f)


def build(dbg=(), stage=99, paranoid=2, dbg_cc=0):
    nc = bass.Bass("TRN2", target_bir_lowering=False)
    dbg = set(dbg)

    xT = nc.dram_tensor("xT", [D, 3 + TC], F32, kind="ExternalInput").ap()
    winT = nc.dram_tensor("winT", [D, PROJ], F32, kind="ExternalInput").ap()
    woutT = nc.dram_tensor("woutT", [D, D], F32, kind="ExternalInput").ap()
    alog = nc.dram_tensor("alog", [1, H], F32, kind="ExternalInput").ap()
    dtbias = nc.dram_tensor("dtbias", [H, 1], F32, kind="ExternalInput").ap()
    cwx_in = nc.dram_tensor("cwx", [128, 16 * KCONV], F32, kind="ExternalInput").ap()
    cwb_in = nc.dram_tensor("cwb", [128, 32 * KCONV], F32, kind="ExternalInput").ap()
    cwc_in = nc.dram_tensor("cwc", [128, 32 * KCONV], F32, kind="ExternalInput").ap()
    gwT = nc.dram_tensor("gwT", [D, DI], I32, kind="ExternalInput").ap()
    uwT = nc.dram_tensor("uwT", [D, DI], I32, kind="ExternalInput").ap()
    dwT = nc.dram_tensor("dwT", [DI, D], I32, kind="ExternalInput").ap()
    gs_in = nc.dram_tensor("gs", [128, DI // 128], F32, kind="ExternalInput").ap()
    us_in = nc.dram_tensor("us", [128, DI // 128], F32, kind="ExternalInput").ap()
    ds_in = nc.dram_tensor("ds", [128, D // 128], F32, kind="ExternalInput").ap()
    onehot = nc.dram_tensor("onehot", [1, NCORE], F32, kind="ExternalInput").ap()
    out_d = nc.dram_tensor("out", [D, TC], F32, kind="ExternalOutput").ap()

    xsp = nc.dram_tensor("xsp", [D, 3 + TC], BF16).ap()
    bsp = nc.dram_tensor("bsp", [H * N, 3 + TC], BF16).ap()
    csp = nc.dram_tensor("csp", [H * N, 3 + TC], BF16).ap()
    lbounce = nc.dram_tensor("lbounce", [NCH, H, Q], F32).ap()
    ar_in = nc.dram_tensor("ar_in", [NCORE, N, H * P], F32).ap()
    ar_out = nc.dram_tensor("ar_out", [NCORE, N, H * P], F32).ap()
    ar2_in = nc.dram_tensor("ar2_in", [NCORE, 1, H], F32).ap()
    ar2_out = nc.dram_tensor("ar2_out", [NCORE, 1, H], F32).ap()

    dbg_t = {}
    for name, shape in (("d_spx", [D, 3 + TC]), ("d_spb", [H * N, 3 + TC]),
                        ("d_spc", [H * N, 3 + TC]), ("d_dtv", [H, TC]),
                        ("d_lam", [H, TC]),
                        ("d_xc", [128, 16, Q]), ("d_bc", [128, 32, Q]),
                        ("d_xin", [128, 16, 131]),
                        ("d_E", [128, 512]), ("d_M", [128, H, 128]),
                        ("d_Y", [128, NCH, H * P]), ("d_h", [N, H * P]),
                        ("d_Hm", [N, H * P]), ("d_zs", [128, NCH, D]),
                        ("d_yg", [128, 16, TC]), ("d_ln1", [128, 16, TC]),
                        ("d_hmid", [128, 64, TC // 2]), ("d_ltok", [128, H]),
                        ("d_yfin", [128, H * P]), ("d_r1", [128, 16, 512]),
                        ("d_h0", [N, H * P]), ("d_gw", [1, H]), ("d_wt", [128, H]),
                        ("d_rms", [128, 3 * H]), ("d_ptl", [128, 3 * H]),
                        ("d_y1", [128, H * P]), ("d_pt1", [128, H]), ("d_hbf", [N, H * P])):
        if name in dbg:
            bf_names = ("d_spx", "d_spb", "d_spc", "d_xc", "d_bc", "d_E", "d_M", "d_xin", "d_hbf", "d_y1",
                        "d_Y", "d_zs", "d_yg", "d_ln1", "d_hmid")
            dt_ = BF16 if name in bf_names else F32
            dbg_t[name] = nc.dram_tensor(name, shape, dt_, kind="ExternalOutput").ap()

    ctx = []

    def sbt(name, shape, dtype):
        t = nc.sbuf_tensor(name, shape, dtype)
        ctx.append(t)
        return t.__enter__()

    def pst(name, shape, dtype=F32):
        t = nc.psum_tensor(name, shape, dtype)
        ctx.append(t)
        return t.__enter__()

    def view(arena, byte_off, shape, dtype):
        size = mybir.dt.size(dtype)
        n = int(np.prod(shape[1:])) * size
        ap = arena.ap()[0:shape[0], byte_off:byte_off + n].bitcast(dtype)
        if len(shape) == 2:
            return ap
        spec = ("p (" + " ".join(f"d{i}" for i in range(1, len(shape))) + ") -> p "
                + " ".join(f"d{i}" for i in range(1, len(shape))))
        kw = {f"d{i}": shape[i] for i in range(1, len(shape))}
        return ap.rearrange(spec, **kw)

    arA = sbt("arA", [128, 32768], U8)
    arB = sbt("arB", [128, 32768], U8)
    arC = sbt("arC", [128, 36864], U8)
    arD = sbt("arD", [128, 32768], U8)
    arF = sbt("arF", [128, 33024], U8)

    ident_bf = sbt("ident_bf", [128, 128], BF16)
    ident_f = sbt("ident_f", [128, 128], F32)
    tri_bf = sbt("tri_bf", [128, 128], BF16)
    ones_f = sbt("ones_f", [1, 128], F32)
    ones_bf = sbt("ones_bf", [1, 128], BF16)
    ones_cf = sbt("ones_cf", [128, 1], F32)
    ones_cb = sbt("ones_cb", [128, 1], BF16)
    oh_sb = sbt("oh_sb", [128, NCORE], F32)
    acol = sbt("acol", [H, 1], F32)
    dtb = sbt("dtb", [H, 1], F32)
    cwx = sbt("cwx_s", [128, 16 * KCONV], F32)
    cwb = sbt("cwb_s", [128, 32 * KCONV], F32)
    cwc = sbt("cwc_s", [128, 32 * KCONV], F32)
    gs_sb = sbt("gs_sb", [128, DI // 128], F32)
    us_sb = sbt("us_sb", [128, DI // 128], F32)
    ds_sb = sbt("ds_sb", [128, D // 128], F32)
    stg = sbt("stg", [128, 512], BF16)
    stg2 = sbt("stg2", [128, 512], BF16)
    fla = sbt("fla", [1, 512], F32)
    flb = sbt("flb", [1, 512], F32)
    ell0 = sbt("ell0", [H, Q], F32)
    ell1 = sbt("ell1", [H, Q], F32)
    wsc = sbt("wsc", [H, Q], F32)
    ltok = sbt("ltok", [128, H], F32)
    vtok = sbt("vtok", [128, H], F32)
    wtok = sbt("wtok", [128, H], F32)
    gsb = sbt("gsb", [128, H], F32)
    ptokf = sbt("ptokf", [128, H], F32)
    grow = sbt("grow", [1, H], F32)
    lamrow = sbt("lamrow", [1, H], F32)
    dcore = sbt("dcore", [1, H], F32)
    drow = sbt("drow", [1, H], F32)
    ptok_all = sbt("ptok_all", [128, NCH, H], BF16)
    lam_all = sbt("lam_all", [1, NCH, H], F32)
    h_sb = sbt("h_sb", [N, H * P], F32)
    h_bf = sbt("h_bf", [N, H * P], BF16)

    ps_a = pst("ps_a", [128, 512])
    ps_b = pst("ps_b", [128, 512])
    ps_c = pst("ps_c", [128, 512])
    ps_d = pst("ps_d", [128, 512])
    ps_e = pst("ps_e", [128, 512])
    ps_f = pst("ps_f", [128, 512])
    ps_g = pst("ps_g", [128, 512])
    ps_t = pst("ps_t", [128, 1024], BF16)

    pr = Prog(nc)
    for s in ("x", "f0", "f1", "st", "mm", "ve", "ac", "gp", "cc", "gxb", "gs0", "gs1",
              "gxi", "gbi", "gci", "gar", "gsl", "g0", "g1", "g2", "g3",
              "g4", "g5"):
        pr.sem(s)
    pr.paranoid = paranoid

    def drain(level, bit=None):
        if bit is not None and paranoid < 0:
            if not ((-paranoid) >> bit) & 1:
                return
        elif paranoid < level:
            return
        if True:
            for e in ("tensor", "scalar", "gpsimd", "sync"):
                pr.wait(e, "ve")
                pr.wait(e, "ac")
            for sname in ("ac", "mm", "x", "f0", "f1", "gs0", "gs1", "gxi", "gbi", "gci",
                          "g0", "g1", "g2", "g3", "g4", "g5", "gxb"):
                pr.wait("vector", sname)
            pr.wait("scalar", "mm")
            pr.wait("tensor", "mm")

    Y_all = view(arA, 0, [128, NCH, H * P], BF16)
    hp_lo = view(arA, 0, [128, 32, TC // 2], BF16)

    btok = view(arB, 0, [128, H * N], BF16)
    dt_pre = view(arB, 8192, [H, BLK], F32)
    dtv = view(arB, 10240, [H, BLK], F32)
    lndt = view(arB, 12288, [H, BLK], F32)
    lam = view(arB, 14336, [H, BLK], F32)
    sfp = view(arB, 16384, [H, BLK], F32)
    wo_a = view(arB, 18432, [128, 16, 128], BF16)
    wo_b = view(arB, 22528, [128, 16, 128], BF16)
    hp_hi = view(arB, 0, [128, 32, TC // 2], BF16)

    cc_sb = view(arC, 0, [128, 32, 128], BF16)
    bc_sb = view(arC, 8192, [128, 32, 128], BF16)
    xcc = view(arC, 16384, [128, 16, 128], BF16)
    xctok = view(arC, 20480, [128, H * P], BF16)
    xin_c = view(arC, 24576, [128, 16, 131], BF16)
    wz_a = view(arC, 0, [128, 16, 512], BF16)
    wz_b = view(arC, 16384, [128, 16, 512], BF16)
    ygT = view(arC, 0, [128, 16, TC], BF16)
    gw_a = view(arC, 0, [128, 16, 128], BF16)
    gw_b = view(arC, 4096, [128, 16, 128], BF16)
    uw_a = view(arC, 8192, [128, 16, 128], BF16)
    uw_b = view(arC, 12288, [128, 16, 128], BF16)
    dw_a = view(arC, 0, [128, 64, 128], BF16)
    dw_b = view(arC, 16384, [128, 64, 128], BF16)

    hmine = view(arB, 0, [N, H * P], F32)
    wina = view(arB, 18432, [128, 16, 128], BF16)
    winb = view(arB, 22528, [128, 16, 128], BF16)
    ysb = stg.ap()
    acca = stg2.ap()[:, 0:256].bitcast(F32)
    accb = stg2.ap()[:, 256:512].bitcast(F32)
    rs0 = view(arD, 29952, [128, H], F32)
    rs1 = view(arD, 30208, [128, H], F32)
    rs2 = view(arD, 30464, [128, H], F32)
    xsq = view(arC, 35840, [128, 512], BF16)
    rln_b3 = (view(arB, 26624, [1, 512], F32), view(arB, 28672, [1, 512], F32),
              view(arB, 30720, [1, 512], F32))
    rln_ffn = (view(arF, 18432, [1, 512], F32), view(arF, 20480, [1, 512], F32),
               view(arF, 22528, [1, 512], F32))
    rs128 = view(arF, 16384, [128, 512], F32)
    outscr = (view(arF, 24576, [128, 512], F32), view(arF, 26624, [128, 512], F32))
    dscr = view(arC, 32768, [128, 512], F32)
    gsil = view(arC, 34816, [128, 512], BF16)
    cin_c = view(arD, 0, [128, 32, 131], BF16)
    bin_c = view(arD, 8448, [128, 32, 131], BF16)
    m_sb = view(arD, 16896, [128, H, 128], BF16)
    e_sb = view(arD, 25088, [128, 512], BF16)
    em_sb = view(arD, 26112, [128, 512], BF16)
    xtb_a = view(arD, 0, [128, 16, 128], BF16)
    xtb_b = view(arD, 4096, [128, 16, 128], BF16)
    yfin = view(arD, 0, [128, H * P], F32)
    ysq = view(arD, 8192, [128, H * P], BF16)
    cin_b2 = view(arD, 12288, [128, 32, 131], BF16)
    cc_b2 = view(arD, 20736, [128, 32, 128], BF16)
    ptl = view(arD, 28928, [128, H], F32)
    lamb = view(arD, 29440, [128, H], F32)
    ln1 = view(arD, 0, [128, 16, TC], BF16)

    XB = view(arF, 0, [128, 16, 515], BF16)
    s_slot = view(arF, 0, [N, H * P], F32)
    hrun = view(arF, 8192, [N, H * P], F32)
    hmask = view(arF, 16384, [N, H * P], F32)
    dbrow = view(arF, 24576, [128, H], F32)
    z_sil = view(arF, 0, [128, NCH, D], BF16)
    XB3 = view(arF, 0, [128, 16, 512], F32)
    resid2 = view(arF, 0, [128, 16, 512], BF16)

    pr.op("gpsimd", "memset", ident_bf.ap(), 0.0, sem="gp")
    pr.op("gpsimd", "memset", ident_f.ap(), 0.0, sem="gp")
    pr.op("gpsimd", "memset", tri_bf.ap(), 0.0, sem="gp")
    pr.wait("gpsimd", "gp")
    pr.op("gpsimd", "affine_select", out=ident_bf.ap(), in_=ident_bf.ap(),
          compare_op=ALU.not_equal, fill=1.0, base=0, pattern=[[-1, 128]],
          channel_multiplier=1, sem="gp")
    pr.op("gpsimd", "affine_select", out=ident_f.ap(), in_=ident_f.ap(),
          compare_op=ALU.not_equal, fill=1.0, base=0, pattern=[[-1, 128]],
          channel_multiplier=1, sem="gp")
    pr.op("gpsimd", "affine_select", out=tri_bf.ap(), in_=tri_bf.ap(),
          compare_op=ALU.is_gt, fill=1.0, base=0, pattern=[[-1, 128]],
          channel_multiplier=1, sem="gp")
    pr.wait("gpsimd", "gp")
    pr.dma("sync", acol.ap(), alog.rearrange("a h -> h a"), "x")
    pr.dma("sync", dtb.ap(), dtbias, "x")
    pr.dma("sync", cwx.ap(), cwx_in, "x")
    pr.dma("sync", cwb.ap(), cwb_in, "x")
    pr.dma("sync", cwc.ap(), cwc_in, "x")
    pr.dma("sync", gs_sb.ap(), gs_in, "x")
    pr.dma("sync", us_sb.ap(), us_in, "x")
    pr.dma("sync", ds_sb.ap(), ds_in, "x")
    t_oh = pr.dma("sync", oh_sb.ap()[0:1, :], onehot, "x")
    pr.wait("vector", "x")
    pr.op("vector", "memset", ones_f.ap(), 1.0)
    pr.op("vector", "memset", ones_bf.ap(), 1.0)
    pr.op("vector", "memset", ones_cf.ap(), 1.0)
    pr.op("vector", "memset", ones_cb.ap(), 1.0)
    pr.op("vector", "memset", h_sb.ap(), 0.0)
    pr.op("vector", "memset", h_bf.ap(), 0.0)
    pr.op("vector", "memset", lamrow.ap(), 1.0)
    pr.wait("scalar", "x")
    pr.op("scalar", "activation", acol.ap(), acol.ap(), AF.Exp, sem="ac")
    pr.wait("vector", "ac")
    pr.op("vector", "tensor_scalar_mul", acol.ap(), acol.ap(), -1.0, sem="ve")
    pr.wait("tensor", "ve")
    pr.op("tensor", "matmul", ps_g.ap()[:, 0:NCORE], ones_f.ap().bitcast(F32),
          oh_sb.ap()[0:1, :].bitcast(F32), start=True, stop=True, sem="mm")
    pr.wait("vector", "mm")
    pr.op("vector", "tensor_copy", oh_sb.ap(), ps_g.ap()[:, 0:NCORE], sem="ve")

    wbufs = [wina, winb]
    stgs = [stg, stg2]
    pbanks = [ps_a, ps_b]
    win_consumed = []
    psum_free = [0, 0]
    stg_free = [0, 0]
    psd_free = [0]
    fl_mm = [0, 0]
    fl_ac = [0, 0]

    sections = [("x", 0, 16, xsp), ("B", 4096, 32, bsp), ("C", 8192, 32, csp)]

    def proj_block(b):
        c0 = BLK * b
        if b == 0:
            t_x = pr.dma("gpsimd", XB[:, :, 0:515],
                         xT[:, 0:515].rearrange("(g p) t -> p g t", p=128), "gxb")
        else:
            t_x = pr.dma("gpsimd", XB[:, :, 3:515],
                         xT[:, 3 + c0:3 + c0 + BLK].rearrange("(g p) t -> p g t",
                                                              p=128), "gxb")
        for sname, obase, n_ot, spill in sections:
            for ot in range(n_ot):
                o0 = obase + ot * 128
                i = len(win_consumed)
                buf = wbufs[i % 2]
                wsem = "g0" if i % 2 == 0 else "g1"
                if i >= 2:
                    pr.wait("gpsimd", "mm", win_consumed[i - 2])
                t_w = pr.dma("gpsimd", buf[:, :, :],
                             winT[:, o0:o0 + 128].rearrange("(g p) o -> p g o",
                                                            p=128), wsem)
                bank = pbanks[i % 2]
                pr.wait("tensor", "ve", psum_free[i % 2])
                pr.wait("tensor", wsem, t_w)
                pr.wait("tensor", "gxb", t_x)
                for k in range(16):
                    pr.op("tensor", "matmul", bank.ap(), buf[:, k, :],
                          XB[:, k, 3:515], start=(k == 0), stop=(k == 15),
                          sem=("mm" if k == 15 else None))
                tmm = pr.val("mm")
                if b == 0:
                    pr.wait("tensor", "ve", psd_free[0])
                    for k in range(16):
                        pr.op("tensor", "matmul", ps_d.ap()[:, 0:3], buf[:, k, :],
                              XB[:, k, 0:3], start=(k == 0), stop=(k == 15),
                              sem=("mm" if k == 15 else None))
                    tmm = pr.val("mm")
                win_consumed.append(tmm)
                st_buf = stgs[i % 2]
                ssem = "gs0" if i % 2 == 0 else "gs1"
                pr.wait("vector", ssem, stg_free[i % 2])
                pr.wait("vector", "mm", tmm)
                tv = pr.op("vector", "tensor_copy", st_buf.ap(), bank.ap(), sem="ve")
                psum_free[i % 2] = tv
                pr.wait("gpsimd", "ve", tv)
                tg = pr.dma("gpsimd", spill[o0 - obase:o0 - obase + 128,
                                            3 + c0:3 + c0 + BLK], st_buf.ap(), ssem)
                if b == 0:
                    pr.wait("vector", ssem, tg)
                    tv2 = pr.op("vector", "tensor_copy", st_buf.ap()[:, 0:3],
                                ps_d.ap()[:, 0:3], sem="ve")
                    psd_free[0] = tv2
                    pr.wait("gpsimd", "ve", tv2)
                    tg = pr.dma("gpsimd",
                                spill[o0 - obase:o0 - obase + 128, 0:3],
                                st_buf.ap()[:, 0:3], ssem)
                stg_free[i % 2] = tg
        i = len(win_consumed)
        buf = wbufs[i % 2]
        wsem = "g0" if i % 2 == 0 else "g1"
        pr.wait("gpsimd", "mm", win_consumed[i - 2])
        t_w = pr.dma("gpsimd", buf[:, :, 0:32],
                     winT[:, 12288:12320].rearrange("(g p) o -> p g o", p=128),
                     wsem)
        pr.wait("tensor", wsem, t_w)
        pr.wait("tensor", "ve", psum_free[i % 2])
        for k in range(16):
            pr.op("tensor", "matmul", pbanks[i % 2].ap()[0:H, :], buf[:, k, 0:32],
                  XB[:, k, 3:515], start=(k == 0), stop=(k == 15),
                  sem=("mm" if k == 15 else None))
        win_consumed.append(pr.val("mm"))
        pr.wait("vector", "mm")
        pr.op("vector", "tensor_scalar_add", dt_pre, pbanks[i % 2].ap()[0:H, :],
              dtb.ap(), sem="ve")
        pr.wait("vector", "ve")
        tv = pr.op("vector", "tensor_copy", sfp, dt_pre, sem="ve")
        psum_free[i % 2] = tv
        pr.wait("scalar", "ve", tv)
        pr.op("scalar", "activation", dtv, dt_pre, AF.Abs, sem="ac")
        pr.wait("scalar", "ac")
        pr.op("scalar", "activation", dtv, dtv, AF.Exp, scale=-1.0, sem="ac")
        pr.wait("scalar", "ac")
        pr.op("scalar", "activation", dtv, dtv, AF.Ln, bias=1.0, sem="ac")
        pr.wait("vector", "ac")
        pr.op("vector", "tensor_relu", lam, dt_pre, sem="ve")
        pr.wait("vector", "ve")
        pr.op("vector", "tensor_add", dtv, dtv, lam, sem="ve")
        pr.wait("scalar", "ve")
        pr.wait("vector", "ve")
        pr.op("scalar", "activation", lndt, dtv, AF.Ln, sem="ac")
        pr.op("vector", "tensor_scalar_mul", lam, dtv, acol.ap(), sem="ve")
        if "d_dtv" in dbg_t:
            pr.wait("sync", "ve")
            pr.dma("sync", dbg_t["d_dtv"][:, c0:c0 + BLK], dtv, "st")
            pr.dump_barrier()
        if "d_lam" in dbg_t:
            pr.wait("sync", "ve")
            pr.dma("sync", dbg_t["d_lam"][:, c0:c0 + BLK], lam, "st")
            pr.dump_barrier()

    def dt_chunk(cc):
        q0 = (cc % 4) * Q
        pr.wait("vector", "ve")
        pr.wait("vector", "ac")
        src, dst = ell0, ell1
        pr.op("vector", "tensor_copy", src.ap(), lam[:, q0:q0 + Q], sem="ve")
        pr.wait("vector", "ve")
        sh = 1
        while sh < Q:
            pr.op("vector", "tensor_copy", dst.ap()[:, 0:sh], src.ap()[:, 0:sh])
            pr.op("vector", "tensor_tensor", dst.ap()[:, sh:Q], src.ap()[:, sh:Q],
                  src.ap()[:, 0:Q - sh], op=ALU.add, sem="ve")
            pr.wait("vector", "ve")
            src, dst = dst, src
            sh *= 2
        ell = src
        tve = pr.op("vector", "tensor_copy", dst.ap()[:, 0:1], ell.ap()[:, 0:1],
                    sem="ve")
        pr.wait("sync", "ve", tve)
        pr.dma("sync", lbounce[cc], ell.ap(), "x")
        pr.wait("vector", "ve")
        pr.op("vector", "tensor_sub", dst.ap(), lndt[:, q0:q0 + Q], ell.ap(),
              sem="ve")
        vff = dst
        pr.wait("scalar", "ve")
        pr.op("scalar", "activation", wsc.ap(), vff.ap(), AF.Exp,
              bias=ell.ap()[:, Q - 1:Q], sem="ac")
        pr.wait("tensor", "ac")
        pr.op("tensor", "transpose", ps_g.ap()[:, 0:H], ell.ap(),
              ident_f.ap()[0:H, 0:H])
        pr.op("tensor", "transpose", ps_g.ap()[:, H:2 * H], lndt[:, q0:q0 + Q],
              ident_f.ap()[0:H, 0:H])
        pr.op("tensor", "transpose", ps_g.ap()[:, 2 * H:3 * H], wsc.ap(),
              ident_f.ap()[0:H, 0:H], sem="mm")
        pr.wait("vector", "mm")
        pr.op("vector", "tensor_copy", ltok.ap(), ps_g.ap()[:, 0:H], sem="ve")
        pr.wait("vector", "ve")
        pr.op("vector", "tensor_sub", vtok.ap(), ps_g.ap()[:, H:2 * H],
              ltok.ap())
        tve = pr.op("vector", "tensor_copy", wtok.ap(), ps_g.ap()[:, 2 * H:3 * H],
                    sem="ve")
        pr.wait("scalar", "ve", tve)
        pr.op("scalar", "activation", ptokf.ap(), ltok.ap(), AF.Exp, sem="ac")
        pr.wait("vector", "ac")
        pr.op("vector", "tensor_copy", ptok_all[:, cc, :], ptokf.ap())
        pr.op("vector", "tensor_copy", lam_all[:, cc, :], lamrow.ap(), sem="ve")
        if "d_ltok" in dbg_t and cc == dbg_cc:
            pr.wait("sync", "ve")
            pr.dma("sync", dbg_t["d_ltok"], ltok.ap(), "st")
            pr.dump_barrier()

    def conv_chunk(cin, ccout, cw, ngrp, t_load):
        pr.wait("vector", t_load[0], t_load[1])
        accs = [acca, accb]
        for g in range(ngrp):
            acc = accs[g % 2]
            pr.wait("vector", "ac")
            for k in range(KCONV):
                if k == 0:
                    pr.op("vector", "tensor_scalar_mul", acc,
                          cin[:, g, 0:Q], cw.ap()[:, g * KCONV:g * KCONV + 1],
                          sem="ve")
                else:
                    pr.op("vector", "scalar_tensor_tensor", acc,
                          cin[:, g, k:k + Q],
                          cw.ap()[:, g * KCONV + k:g * KCONV + k + 1],
                          acc, op0=ALU.mult, op1=ALU.add, sem="ve")
                pr.wait("vector", "ve")
            pr.wait("scalar", "ve")
            pr.op("scalar", "activation", ccout[:, g, :], acc, AF.Silu,
                  sem="ac")

    def ssd_chunk(cc):
        t0 = cc * Q
        pr.wait("gpsimd", "gs0")
        pr.wait("gpsimd", "gs1")
        tl_x = pr.dma("gpsimd", xin_c,
                      xsp[:, t0:t0 + 131].rearrange("(g p) t -> p g t", p=128), "gxi")
        tl_b = pr.dma("gpsimd", bin_c,
                      bsp[:, t0:t0 + 131].rearrange("(g p) t -> p g t", p=128), "gbi")
        tl_c = pr.dma("gpsimd", cin_c,
                      csp[:, t0:t0 + 131].rearrange("(g p) t -> p g t", p=128), "gci")
        if "d_hbf" in dbg_t and cc == 1:
            pr.wait("sync", "ve")
            pr.dma("sync", dbg_t["d_hbf"], h_bf.ap(), "st")
            pr.dma("sync", dbg_t["d_h0"], h_sb.ap(), "st")
            pr.dump_barrier()
        drain(2, bit=0)
        dt_chunk(cc)
        drain(2, bit=1)
        conv_chunk(xin_c, xcc, cwx, 16, ("gxi", tl_x))
        conv_chunk(bin_c, bc_sb, cwb, 32, ("gbi", tl_b))
        conv_chunk(cin_c, cc_sb, cwc, 32, ("gci", tl_c))
        drain(2, bit=2)
        if "d_xin" in dbg_t and cc == dbg_cc:
            pr.wait("sync", "gxi", tl_x)
            pr.dma("sync", dbg_t["d_xin"], xin_c, "st")
            pr.dump_barrier()
        if "d_xc" in dbg_t and cc == dbg_cc:
            pr.wait("sync", "ac")
            pr.dma("sync", dbg_t["d_xc"], xcc, "st")
            pr.dump_barrier()
        if "d_bc" in dbg_t and cc == dbg_cc:
            pr.wait("sync", "ac")
            pr.dma("sync", dbg_t["d_bc"], bc_sb, "st")
            pr.dump_barrier()

        pr.wait("tensor", "ac")
        for half in range(2):
            for i2 in range(8):
                g = half * 8 + i2
                pr.op("tensor", "transpose", ps_t.ap()[:, i2 * 128:(i2 + 1) * 128],
                      xcc[:, g, :], ident_bf.ap(), sem=("mm" if i2 == 7 else None))
            pr.wait("vector", "mm")
            tve = pr.op("vector", "tensor_copy",
                        xctok[:, half * 1024:(half + 1) * 1024], ps_t.ap(),
                        sem="ve")
            pr.wait("tensor", "ve", tve)
        for quad in range(4):
            for i2 in range(8):
                g = quad * 8 + i2
                pr.op("tensor", "transpose", ps_t.ap()[:, i2 * 128:(i2 + 1) * 128],
                      bc_sb[:, g, :], ident_bf.ap(), sem=("mm" if i2 == 7 else None))
            pr.wait("vector", "mm")
            tve = pr.op("vector", "tensor_tensor",
                        btok.rearrange("p (h n) -> p h n", h=H)
                        [:, quad * 8:(quad + 1) * 8, :],
                        ps_t.ap().rearrange("p (h n) -> p h n", h=8),
                        wtok.ap()[:, quad * 8:(quad + 1) * 8].unsqueeze(2)
                        .broadcast_to([128, 8, 128]), op=ALU.mult, sem="ve")
            pr.wait("tensor", "ve", tve)

        drain(2, bit=3)
        flbufs = [fla, flb]
        for grp in range(8):
            fb = flbufs[grp % 2]
            fsem = "f0" if grp % 2 == 0 else "f1"
            pr.wait("sync", "mm", fl_mm[grp % 2])
            pr.wait("sync", "ac", fl_ac[grp % 2])
            t_fl = pr.dma("sync", fb.ap(),
                          lbounce[cc].rearrange("h q -> (h q)").unsqueeze(0)
                          [0:1, grp * 512:(grp + 1) * 512], fsem)
            pr.wait("scalar", fsem, t_fl)
            fl_ac[grp % 2] = pr.op("scalar", "activation",
                                   grow.ap()[0:1, grp * 4:(grp + 1) * 4],
                                   fb.ap()[0:1, 127::128], AF.Exp, sem="ac")
            pr.wait("tensor", fsem, t_fl)
            pr.wait("tensor", "ve")
            tmm = pr.op("tensor", "matmul", ps_c.ap(), ones_f.ap().bitcast(F32),
                        fb.ap().bitcast(F32), start=True, stop=True, sem="mm")
            fl_mm[grp % 2] = tmm
            pr.wait("scalar", "mm", tmm)
            for hh in range(4):
                h = grp * 4 + hh
                pr.op("scalar", "activation", e_sb[:, hh * 128:(hh + 1) * 128],
                      ps_c.ap()[:, hh * 128:(hh + 1) * 128], AF.Exp,
                      bias=vtok.ap()[:, h:h + 1], sem=("ac" if hh == 3 else None))
            pr.wait("tensor", "ac")
            for hh in range(4):
                h = grp * 4 + hh
                pr.op("tensor", "matmul", ps_d.ap()[:, hh * 128:(hh + 1) * 128],
                      bc_sb[:, h, :], cc_sb[:, h, :], start=(hh == 0),
                      stop=(hh == 3), sem=("mm" if hh == 3 else None))
            pr.wait("vector", "ac")
            pr.op("vector", "tensor_tensor",
                  em_sb.rearrange("p (r i) -> p r i", r=4),
                  e_sb.rearrange("p (r i) -> p r i", r=4),
                  tri_bf.ap().unsqueeze(1).broadcast_to([128, 4, 128]),
                  op=ALU.mult, sem="ve")
            pr.wait("vector", "ve")
            pr.wait("vector", "mm")
            pr.op("vector", "tensor_tensor", m_sb[:, grp * 4:(grp + 1) * 4, :],
                  ps_d.ap().rearrange("p (r i) -> p r i", r=4),
                  em_sb.rearrange("p (r i) -> p r i", r=4), op=ALU.mult, sem="ve")
        if "d_E" in dbg_t and cc == dbg_cc:
            pr.wait("sync", "ve")
            pr.dma("sync", dbg_t["d_E"], em_sb, "st")
            pr.dump_barrier()
        if "d_M" in dbg_t and cc == dbg_cc:
            pr.wait("sync", "ve")
            pr.dma("sync", dbg_t["d_M"], m_sb, "st")
            pr.dump_barrier()

        drain(2, bit=4)
        pr.wait("vector", "ac")
        t_lam = pr.op("vector", "tensor_tensor", lamrow.ap(), lamrow.ap(),
                      grow.ap(), op=ALU.mult, sem="ve")
        pr.wait("tensor", "ve", t_lam)
        tmm = pr.op("tensor", "matmul", ps_g.ap()[:, 0:H],
                    ones_f.ap().bitcast(F32), grow.ap().bitcast(F32),
                    start=True, stop=True, sem="mm")
        pr.wait("vector", "mm", tmm)
        pr.op("vector", "tensor_copy", gsb.ap(), ps_g.ap()[:, 0:H], sem="ve")

        for grp in range(4):
            h0 = grp * 8
            pr.wait("tensor", "ve")
            for hh in range(8):
                h = h0 + hh
                pr.op("tensor", "matmul", ps_e.ap()[:, hh * 64:(hh + 1) * 64],
                      m_sb[:, h, :], xctok[:, h * 64:(h + 1) * 64],
                      start=(hh == 0), stop=(hh == 7))
            for hh in range(8):
                h = h0 + hh
                pr.op("tensor", "matmul", ps_f.ap()[:, hh * 64:(hh + 1) * 64],
                      cc_sb[:, h, :], h_bf.ap()[:, h * 64:(h + 1) * 64],
                      start=(hh == 0), stop=(hh == 7),
                      sem=("mm" if hh == 7 else None))
            tmm = pr.val("mm")
            pr.wait("vector", "mm", tmm)
            pr.op("vector", "tensor_copy", ysb, ps_e.ap(), sem="ve")
            pr.wait("vector", "ve")
            for hh in range(8):
                h = h0 + hh
                pr.op("vector", "scalar_tensor_tensor",
                      Y_all[:, cc, h * 64:(h + 1) * 64],
                      ps_f.ap()[:, hh * 64:(hh + 1) * 64], ptokf.ap()[:, h:h + 1],
                      ysb[:, hh * 64:(hh + 1) * 64],
                      op0=ALU.mult, op1=ALU.add, sem=("ve" if hh == 7 else None))
        drain(2, bit=5)
        pr.op("vector", "tensor_tensor",
              h_sb.ap().rearrange("p (h q) -> p h q", h=H),
              h_sb.ap().rearrange("p (h q) -> p h q", h=H),
              gsb.ap().unsqueeze(2).broadcast_to([N, H, P]), op=ALU.mult, sem="ve")
        pr.wait("vector", "ve")
        for grp in range(4):
            h0 = grp * 8
            pr.wait("tensor", "ve")
            for hh in range(8):
                h = h0 + hh
                pr.op("tensor", "matmul", ps_e.ap()[:, hh * 64:(hh + 1) * 64],
                      btok[:, h * 128:(h + 1) * 128],
                      xctok[:, h * 64:(h + 1) * 64],
                      start=(hh == 0), stop=(hh == 7),
                      sem=("mm" if hh == 7 else None))
            pr.wait("vector", "mm")
            pr.op("vector", "tensor_add", h_sb.ap()[:, h0 * 64:(h0 + 8) * 64],
                  h_sb.ap()[:, h0 * 64:(h0 + 8) * 64], ps_e.ap(), sem="ve")
        pr.wait("vector", "ve")
        pr.op("vector", "tensor_copy", h_bf.ap(), h_sb.ap(), sem="ve")

        if "d_gw" in dbg_t and cc == 0:
            pr.wait("sync", "ve")
            pr.dma("sync", dbg_t["d_gw"], grow.ap(), "st")
            pr.dma("sync", dbg_t["d_wt"], wtok.ap(), "st")
            pr.dump_barrier()
        if "d_y1" in dbg_t and cc == 1:
            pr.wait("sync", "ve")
            pr.dma("sync", dbg_t["d_y1"], Y_all[:, 1, :], "st")
            pr.dma("sync", dbg_t["d_pt1"], ptokf.ap(), "st")
            pr.dump_barrier()
        if "d_Y" in dbg_t and cc == NCH - 1:
            pr.wait("sync", "ve")
            pr.dma("sync", dbg_t["d_Y"], Y_all, "st")
            pr.dump_barrier()
        if "d_h" in dbg_t and cc == NCH - 1:
            pr.wait("sync", "ve")
            pr.dma("sync", dbg_t["d_h"], h_sb.ap(), "st")
            pr.dump_barrier()

    def phase_drain():
        for e in ("tensor", "scalar", "gpsimd", "sync"):
            pr.wait(e, "ve")
            pr.wait(e, "ac")
        for sname in ("ac", "mm", "x", "f0", "f1", "gs0", "gs1", "gxi", "gbi",
                      "gci", "gar", "gsl", "gxb", "g0", "g1", "g2", "g3",
                      "g4", "g5", "st", "cc"):
            pr.wait("vector", sname)
        pr.wait("scalar", "mm")
        pr.wait("tensor", "mm")
        pr.wait("gpsimd", "mm")
        pr.wait("sync", "mm")

    def relay():
        pr.op("vector", "tensor_copy", dcore.ap(), lamrow.ap(), sem="ve")
        pr.wait("vector", "ve")
        pr.op("vector", "memset", hmine, 0.0)
        t_wr = 0
        for s in range(NCORE):
            pr.wait("vector", "gar", t_wr)
            pr.op("vector", "tensor_scalar_mul", hmask, h_sb.ap(),
                  oh_sb.ap()[:, s:s + 1])
            tve = pr.op("vector", "tensor_scalar_mul", drow.ap(), dcore.ap(),
                        oh_sb.ap()[0:1, s:s + 1], sem="ve")
            pr.wait("gpsimd", "ve", tve)
            pr.dma("gpsimd", ar_in[s], hmask, "gar")
            t_wr = pr.dma("gpsimd", ar2_in[s], drow.ap(), "gar")
        pr.wait("gpsimd", "gar")
        pr.op("gpsimd", "collective_compute", "AllReduce", ALU.add,
              replica_groups=[list(range(NCORE))], ins=[ar_in.opt()],
              outs=[ar_out.opt()], sem="cc")
        pr.op("gpsimd", "collective_compute", "AllReduce", ALU.add,
              replica_groups=[list(range(NCORE))], ins=[ar2_in.opt()],
              outs=[ar2_out.opt()], sem="cc")
        pr.wait("gpsimd", "cc")
        pr.op("vector", "memset", hrun, 0.0, sem="ve")
        for k in range(NCORE - 1):
            pr.wait("gpsimd", "ve")
            pr.wait("gpsimd", "mm")
            t_s = pr.dma("gpsimd", s_slot, ar_out[k], "gsl")
            t_d = pr.dma("gpsimd", drow.ap(), ar2_out[k], "gsl")
            pr.wait("tensor", "gsl", t_d)
            pr.wait("tensor", "ve")
            tmm = pr.op("tensor", "matmul", ps_g.ap()[:, 0:H],
                        ones_f.ap().bitcast(F32), drow.ap().bitcast(F32),
                        start=True, stop=True, sem="mm")
            pr.wait("vector", "mm", tmm)
            pr.op("vector", "tensor_copy", dbrow, ps_g.ap()[:, 0:H], sem="ve")
            pr.wait("vector", "ve")
            pr.op("vector", "tensor_tensor",
                  hrun.rearrange("p (h q) -> p h q", h=H),
                  hrun.rearrange("p (h q) -> p h q", h=H),
                  dbrow.unsqueeze(2).broadcast_to([N, H, P]), op=ALU.mult,
                  sem="ve")
            pr.wait("vector", "ve")
            pr.wait("vector", "gsl", t_s)
            pr.op("vector", "tensor_add", hrun, hrun, s_slot, sem="ve")
            pr.wait("vector", "ve")
            pr.op("vector", "scalar_tensor_tensor", hmine, hrun,
                  oh_sb.ap()[:, k + 1:k + 2], hmine,
                  op0=ALU.mult, op1=ALU.add, sem="ve")
            pr.wait("vector", "ve")
        pr.op("vector", "tensor_copy", h_bf.ap(), hmine, sem="ve")
        if "d_Hm" in dbg_t:
            pr.wait("sync", "ve")
            pr.dma("sync", dbg_t["d_Hm"], hmine, "st")
            pr.dump_barrier()

    def pass_b1():
        wzb = [wz_a, wz_b]
        xtb = [xtb_a, xtb_b]
        wz_cons = [0, 0]
        xt_cons = [0, 0]
        for zo in range(4):
            o0 = D + zo * 512
            zsem = "g2" if zo % 2 == 0 else "g3"
            pr.wait("gpsimd", "mm", wz_cons[zo % 2])
            t_wz = pr.dma("gpsimd", wzb[zo % 2].ap() if False else wzb[zo % 2],
                          winT[:, o0:o0 + 512]
                          .rearrange("(g p) o -> p g o", p=128), zsem)
            for cc in range(NCH):
                i = zo * NCH + cc
                xsem = "g0" if i % 2 == 0 else "g1"
                pr.wait("gpsimd", "mm", xt_cons[i % 2])
                t_xt = pr.dma("gpsimd", xtb[i % 2],
                              xT[:, 3 + cc * Q:3 + (cc + 1) * Q]
                              .rearrange("(g p) t -> p g t", p=128), xsem)
                pr.wait("tensor", zsem, t_wz)
                pr.wait("tensor", xsem, t_xt)
                pr.wait("tensor", "ac")
                for k in range(16):
                    pr.op("tensor", "matmul", ps_a.ap(), xtb[i % 2][:, k, :],
                          wzb[zo % 2][:, k, :], start=(k == 0), stop=(k == 15),
                          sem=("mm" if k == 15 else None))
                tmm = pr.val("mm")
                wz_cons[zo % 2] = tmm
                xt_cons[i % 2] = tmm
                pr.wait("scalar", "mm", tmm)
                pr.op("scalar", "activation", z_sil[:, cc, zo * 512:(zo + 1) * 512],
                      ps_a.ap(), AF.Silu, sem="ac")
        if "d_zs" in dbg_t:
            pr.wait("sync", "ac")
            pr.dma("sync", dbg_t["d_zs"], z_sil, "st")
            pr.dump_barrier()

    def pass_b2():
        for cc in range(NCH):
            t0 = cc * Q
            pr.wait("gpsimd", "ve")
            tl_c = pr.dma("gpsimd", cin_b2,
                          csp[:, t0:t0 + 131].rearrange("(g p) t -> p g t", p=128),
                          "gci")
            conv_chunk(cin_b2, cc_b2, cwc, 32, ("gci", tl_c))
            lrow = lam_all[0:1, cc, :]
            pr.wait("tensor", "ve")
            tmm = pr.op("tensor", "matmul", ps_g.ap()[:, 0:H],
                        ones_f.ap().bitcast(F32), lrow.bitcast(F32),
                        start=True, stop=True, sem="mm")
            pr.wait("vector", "mm", tmm)
            pr.op("vector", "tensor_copy", lamb, ps_g.ap()[:, 0:H], sem="ve")
            pr.wait("vector", "ve")
            pr.op("vector", "tensor_tensor", ptl, ptok_all[:, cc, :], lamb,
                  op=ALU.mult, sem="ve")
            pr.wait("vector", "ve")
            pr.wait("tensor", "ac")
            for grp in range(4):
                h0 = grp * 8
                pr.wait("tensor", "ve")
                for hh in range(8):
                    h = h0 + hh
                    pr.op("tensor", "matmul", ps_e.ap()[:, hh * 64:(hh + 1) * 64],
                          cc_b2[:, h, :], h_bf.ap()[:, h * 64:(h + 1) * 64],
                          start=(hh == 0), stop=(hh == 7),
                          sem=("mm" if hh == 7 else None))
                pr.wait("vector", "mm")
                for hh in range(8):
                    h = h0 + hh
                    pr.op("vector", "scalar_tensor_tensor",
                          yfin[:, h * 64:(h + 1) * 64],
                          ps_e.ap()[:, hh * 64:(hh + 1) * 64], ptl[:, h:h + 1],
                          Y_all[:, cc, h * 64:(h + 1) * 64],
                          op0=ALU.mult, op1=ALU.add,
                          sem=("ve" if hh == 7 else None))
            if "d_yfin" in dbg_t and cc == dbg_cc:
                pr.wait("sync", "ve")
                pr.dma("sync", dbg_t["d_yfin"], yfin, "st")
                pr.dma("sync", dbg_t["d_ptl"][:, 0:H], ptl, "st")
                pr.dma("sync", dbg_t["d_ptl"][:, H:2*H], lamb, "st")
                pr.dma("gpsimd", dbg_t["d_ptl"][:, 2*H:3*H], ptok_all[:, cc, :], "gar")
                pr.dump_barrier()
            pr.wait("vector", "mm")
            pr.wait("vector", "ve")
            pr.op("vector", "tensor_tensor", ysq, yfin, yfin, op=ALU.mult, sem="ve")
            pr.wait("vector", "ve")
            pr.op("vector", "tensor_reduce", rs0,
                  ysq.rearrange("p (h q) -> p h q", h=H), axis=AX.X, op=ALU.add,
                  sem="ve")
            pr.wait("vector", "ve")
            tve = pr.op("vector", "tensor_scalar", rs1, rs0, 1.0 / P, 1e-6,
                        op0=ALU.mult, op1=ALU.add, sem="ve")
            pr.wait("scalar", "ve", tve)
            pr.op("scalar", "activation", rs2, rs1, AF.Ln, sem="ac")
            pr.wait("scalar", "ac")
            pr.op("scalar", "activation", rs0, rs2, AF.Exp, scale=-0.5, sem="ac")
            pr.wait("vector", "ac")
            r = rs0
            if "d_rms" in dbg_t and cc == dbg_cc:
                pr.wait("sync", "ac")
                pr.dma("sync", dbg_t["d_rms"][:, 0:H], rs1, "st")
                pr.dma("sync", dbg_t["d_rms"][:, H:2*H], rs2, "st")
                pr.dma("sync", dbg_t["d_rms"][:, 2*H:3*H], rs0, "st")
                pr.dump_barrier()
            pr.op("vector", "tensor_tensor",
                  ysq.rearrange("p (h q) -> p h q", h=H),
                  yfin.rearrange("p (h q) -> p h q", h=H),
                  r.unsqueeze(2).broadcast_to([128, H, P]), op=ALU.mult, sem="ve")
            pr.wait("vector", "ve")
            pr.op("vector", "tensor_tensor", ysq, ysq, z_sil[:, cc, :],
                  op=ALU.mult, sem="ve")
            pr.wait("vector", "ve")
            pr.wait("tensor", "ve")
            for half in range(2):
                for i2 in range(8):
                    g = half * 8 + i2
                    pr.op("tensor", "transpose",
                          ps_t.ap()[:, i2 * 128:(i2 + 1) * 128],
                          ysq.rearrange("p (g i) -> p g i", g=16)[:, g, :],
                          ident_bf.ap(), sem=("mm" if i2 == 7 else None))
                pr.wait("vector", "mm")
                tve = pr.op("vector", "tensor_copy",
                            ygT[:, half * 8:(half + 1) * 8, t0:t0 + Q],
                            ps_t.ap().rearrange("p (g i) -> p g i", g=8),
                            sem="ve")
                pr.wait("tensor", "ve", tve)
        if "d_yg" in dbg_t:
            pr.wait("sync", "ve")
            pr.dma("sync", dbg_t["d_yg"], ygT, "st")
            pr.dump_barrier()

    def emit_ln(xv, out, th, rb, is_bf):
        mu, var, rr = rb
        pr.wait("tensor", "ve")
        for g in range(16):
            if is_bf:
                pr.op("tensor", "matmul", ps_c.ap()[0:1, :], ones_cb.ap(),
                      xv[:, g, :], start=(g == 0), stop=(g == 15),
                      sem=("mm" if g == 15 else None))
            else:
                pr.op("tensor", "matmul", ps_c.ap()[0:1, :],
                      ones_cf.ap(), xv[:, g, :],
                      start=(g == 0), stop=(g == 15),
                      sem=("mm" if g == 15 else None))
        for g in range(16):
            if g:
                pr.wait("vector", "mm")
            pr.op("vector", "tensor_tensor", xsq, xv[:, g, :], xv[:, g, :],
                  op=ALU.mult, sem="ve")
            pr.wait("tensor", "ve")
            pr.op("tensor", "matmul", ps_d.ap()[0:1, :], ones_cb.ap(),
                  xsq, start=(g == 0), stop=(g == 15), sem="mm")
        pr.wait("vector", "mm")
        pr.op("vector", "tensor_scalar_mul", mu, ps_c.ap()[0:1, :], 1.0 / D,
              sem="ve")
        pr.op("vector", "tensor_scalar_mul", var, ps_d.ap()[0:1, :], 1.0 / D,
              sem="ve")
        pr.wait("vector", "ve")
        pr.op("vector", "tensor_tensor", rr, mu, mu, op=ALU.mult, sem="ve")
        pr.wait("vector", "ve")
        pr.op("vector", "tensor_sub", var, var, rr, sem="ve")
        pr.wait("vector", "ve")
        pr.op("vector", "tensor_scalar_add", var, var, 1e-5, sem="ve")
        pr.wait("vector", "ve")
        tve = pr.op("vector", "tensor_copy", fla.ap()[0:1, :], var, sem="ve")
        pr.wait("scalar", "ve", tve)
        pr.op("scalar", "activation", rr, fla.ap()[0:1, :], AF.Ln, sem="ac")
        pr.wait("scalar", "ac")
        pr.op("scalar", "activation", rr, rr, AF.Exp, scale=-0.5, sem="ac")
        pr.wait("vector", "ac")
        tve = pr.op("vector", "tensor_tensor", mu, mu, rr, op=ALU.mult,
                    sem="ve")
        pr.wait("tensor", "ve", tve)
        pr.op("tensor", "matmul", ps_c.ap(), ones_f.ap().bitcast(F32),
              rr.bitcast(F32), start=True, stop=True)
        pr.op("tensor", "matmul", ps_d.ap(), ones_f.ap().bitcast(F32),
              mu.bitcast(F32), start=True, stop=True, sem="mm")
        pr.wait("vector", "mm")
        for g in range(16):
            pr.op("vector", "tensor_tensor", xsq, xv[:, g, :], ps_c.ap(),
                  op=ALU.mult, sem="ve")
            pr.wait("vector", "ve")
            pr.op("vector", "tensor_sub", out[:, g, th * 512:th * 512 + 512],
                  xsq, ps_d.ap(), sem="ve")
            pr.wait("vector", "ve")

    def pass_b3():
        wob = [wo_a, wo_b]
        wo_cons = [0, 0]
        for th in range(2):
            c0 = th * 512
            pr.wait("sync", "ve")
            t_x = pr.dma("sync", XB3,
                         xT[:, 3 + c0:3 + c0 + 512]
                         .rearrange("(g p) t -> p g t", p=128), "x")
            for o in range(16):
                i = th * 16 + o
                osem = "g0" if i % 2 == 0 else "g1"
                pr.wait("gpsimd", "mm", wo_cons[i % 2])
                t_w = pr.dma("gpsimd", wob[i % 2],
                             woutT[:, o * 128:(o + 1) * 128]
                             .rearrange("(g p) o -> p g o", p=128), osem)
                pr.wait("tensor", osem, t_w)
                pr.wait("tensor", "ve")
                for k in range(16):
                    pr.op("tensor", "matmul", ps_e.ap(), wob[i % 2][:, k, :],
                          ygT[:, k, c0:c0 + 512], start=(k == 0), stop=(k == 15),
                          sem=("mm" if k == 15 else None))
                tmm = pr.val("mm")
                wo_cons[i % 2] = tmm
                pr.wait("vector", "mm", tmm)
                pr.wait("vector", "x", t_x)
                pr.op("vector", "tensor_add", XB3[:, o, :], ps_e.ap(),
                      XB3[:, o, :], sem="ve")
            if "d_r1" in dbg_t and th == 0:
                pr.wait("sync", "ve")
                pr.dma("sync", dbg_t["d_r1"], XB3, "st")
                pr.dump_barrier()
            emit_ln(XB3, ln1, th, rln_b3, False)
        if "d_ln1" in dbg_t:
            pr.wait("sync", "ve")
            pr.dma("sync", dbg_t["d_ln1"], ln1, "st")
            pr.dump_barrier()

    def ffn():
        gwb = [gw_a, gw_b]
        uwb = [uw_a, uw_b]
        dwb = [dw_a, dw_b]
        for th in range(2):
            c0 = th * 512
            phase_drain()
            pr.wait("tensor", "ve")
            for g in range(16):
                pr.op("tensor", "matmul", ps_g.ap()[0:1, :], ones_cb.ap(),
                      ln1[:, g, c0:c0 + 512], start=(g == 0), stop=(g == 15),
                      sem=("mm" if g == 15 else None))
            pr.wait("vector", "mm")
            tve = pr.op("vector", "tensor_scalar_mul", flb.ap(),
                        ps_g.ap()[0:1, :], -128.0, sem="ve")
            pr.wait("tensor", "ve", tve)
            tmm = pr.op("tensor", "matmul", ps_g.ap(), ones_f.ap().bitcast(F32),
                        flb.ap().bitcast(F32), start=True, stop=True, sem="mm")
            pr.wait("vector", "mm", tmm)
            pr.op("vector", "tensor_copy", rs128, ps_g.ap(), sem="ve")
            gu_cons = [pr.val("mm"), pr.val("mm")]
            for it in range(DI // 128):
                o0 = it * 128
                gsem = "g0" if it % 2 == 0 else "g1"
                usem = "g2" if it % 2 == 0 else "g3"
                pr.wait("gpsimd", "mm", gu_cons[it % 2])
                t_g = pr.dma("gpsimd", gwb[it % 2],
                             gwT[:, o0:o0 + 128].rearrange("(g p) o -> p g o",
                                                           p=128), gsem)
                t_u = pr.dma("gpsimd", uwb[it % 2],
                             uwT[:, o0:o0 + 128].rearrange("(g p) o -> p g o",
                                                           p=128), usem)
                pr.wait("tensor", gsem, t_g)
                pr.wait("tensor", usem, t_u)
                pr.wait("tensor", "ve")
                for k in range(16):
                    pr.op("tensor", "matmul", ps_a.ap(), gwb[it % 2][:, k, :],
                          ln1[:, k, c0:c0 + 512], start=(k == 0), stop=(k == 15))
                for k in range(16):
                    pr.op("tensor", "matmul", ps_b.ap(), uwb[it % 2][:, k, :],
                          ln1[:, k, c0:c0 + 512], start=(k == 0), stop=(k == 15),
                          sem=("mm" if k == 15 else None))
                tmm = pr.val("mm")
                gu_cons[it % 2] = tmm
                pr.wait("vector", "mm", tmm)
                pr.op("vector", "tensor_add", dscr, ps_a.ap(), rs128,
                      sem="ve")
                pr.wait("scalar", "ve")
                pr.op("scalar", "activation", gsil, dscr, AF.Silu,
                      scale=gs_sb.ap()[:, it:it + 1], sem="ac")
                pr.wait("vector", "ac")
                pr.op("vector", "tensor_add", dscr, ps_b.ap(), rs128, sem="ve")
                pr.wait("vector", "ve")
                hp = hp_lo if it < 32 else hp_hi
                pr.op("vector", "scalar_tensor_tensor",
                      hp[:, it % 32, :], dscr,
                      us_sb.ap()[:, it:it + 1], gsil,
                      op0=ALU.mult, op1=ALU.mult, sem="ve")
            if "d_hmid" in dbg_t and th == 0:
                pr.wait("sync", "ve")
                for g in range(32):
                    pr.dma("sync", dbg_t["d_hmid"][:, g, :], hp_lo[:, g, :], "st")
                    pr.dump_barrier()
                for g in range(32):
                    pr.dma("sync", dbg_t["d_hmid"][:, 32 + g, :], hp_hi[:, g, :],
                           "st")
            pr.wait("tensor", "ve")
            for g in range(64):
                hp = hp_lo if g < 32 else hp_hi
                pr.op("tensor", "matmul", ps_g.ap()[0:1, :], ones_cb.ap(),
                      hp[:, g % 32, :], start=(g == 0), stop=(g == 63),
                      sem=("mm" if g == 63 else None))
            pr.wait("vector", "mm")
            tve = pr.op("vector", "tensor_scalar_mul", flb.ap(),
                        ps_g.ap()[0:1, :], -128.0, sem="ve")
            pr.wait("tensor", "ve", tve)
            tmm = pr.op("tensor", "matmul", ps_g.ap(), ones_f.ap().bitcast(F32),
                        flb.ap().bitcast(F32), start=True, stop=True, sem="mm")
            pr.wait("vector", "mm", tmm)
            pr.op("vector", "tensor_copy", rs128, ps_g.ap(), sem="ve")
            dw_cons = [pr.val("mm"), pr.val("mm")]
            for ot in range(16):
                o0 = ot * 128
                dsem = "g4" if ot % 2 == 0 else "g5"
                pr.wait("gpsimd", "mm", dw_cons[ot % 2])
                t_d1 = pr.dma("gpsimd", dwb[ot % 2][:, 0:32, :],
                              dwT[0:4096, o0:o0 + 128]
                              .rearrange("(g p) o -> p g o", p=128), dsem)
                t_d2 = pr.dma("gpsimd", dwb[ot % 2][:, 32:64, :],
                              dwT[4096:8192, o0:o0 + 128]
                              .rearrange("(g p) o -> p g o", p=128), dsem)
                pr.wait("tensor", dsem, t_d2)
                pr.wait("tensor", "ve")
                for k in range(64):
                    hp = hp_lo if k < 32 else hp_hi
                    pr.op("tensor", "matmul", ps_a.ap(), dwb[ot % 2][:, k, :],
                          hp[:, k % 32, :], start=(k == 0),
                          stop=(k == 63), sem=("mm" if k == 63 else None))
                tmm = pr.val("mm")
                dw_cons[ot % 2] = tmm
                pr.wait("vector", "mm", tmm)
                pr.op("vector", "tensor_add", dscr, ps_a.ap(), rs128, sem="ve")
                pr.wait("vector", "ve")
                pr.op("vector", "scalar_tensor_tensor", resid2[:, ot, :],
                      dscr, ds_sb.ap()[:, ot:ot + 1],
                      ln1[:, ot, c0:c0 + 512], op0=ALU.mult, op1=ALU.add,
                      sem="ve")
            emit_ln2_store(th)

    def emit_ln2_store(th):
        mu, var, rr = rln_ffn
        pr.wait("tensor", "ve")
        for g in range(16):
            pr.op("tensor", "matmul", ps_c.ap()[0:1, :], ones_cb.ap(),
                  resid2[:, g, :], start=(g == 0), stop=(g == 15),
                  sem=("mm" if g == 15 else None))
        for g in range(16):
            if g:
                pr.wait("vector", "mm")
            pr.op("vector", "tensor_tensor", xsq, resid2[:, g, :],
                  resid2[:, g, :], op=ALU.mult, sem="ve")
            pr.wait("tensor", "ve")
            pr.op("tensor", "matmul", ps_d.ap()[0:1, :], ones_cb.ap(),
                  xsq, start=(g == 0), stop=(g == 15), sem="mm")
        pr.wait("vector", "mm")
        pr.op("vector", "tensor_scalar_mul", mu, ps_c.ap()[0:1, :], 1.0 / D,
              sem="ve")
        pr.op("vector", "tensor_scalar_mul", var, ps_d.ap()[0:1, :], 1.0 / D,
              sem="ve")
        pr.wait("vector", "ve")
        pr.op("vector", "tensor_tensor", rr, mu, mu, op=ALU.mult, sem="ve")
        pr.wait("vector", "ve")
        pr.op("vector", "tensor_sub", var, var, rr, sem="ve")
        pr.wait("vector", "ve")
        pr.op("vector", "tensor_scalar_add", var, var, 1e-5, sem="ve")
        pr.wait("vector", "ve")
        tve = pr.op("vector", "tensor_copy", fla.ap()[0:1, :], var, sem="ve")
        pr.wait("scalar", "ve", tve)
        pr.op("scalar", "activation", rr, fla.ap()[0:1, :], AF.Ln, sem="ac")
        pr.wait("scalar", "ac")
        pr.op("scalar", "activation", rr, rr, AF.Exp, scale=-0.5, sem="ac")
        pr.wait("vector", "ac")
        tve = pr.op("vector", "tensor_tensor", mu, mu, rr, op=ALU.mult,
                    sem="ve")
        pr.wait("tensor", "ve", tve)
        pr.op("tensor", "matmul", ps_c.ap(), ones_f.ap().bitcast(F32),
              rr.bitcast(F32), start=True, stop=True)
        pr.op("tensor", "matmul", ps_d.ap(), ones_f.ap().bitcast(F32),
              mu.bitcast(F32), start=True, stop=True, sem="mm")
        pr.wait("vector", "mm")
        st_done = [0, 0]
        for g in range(16):
            osb = outscr[g % 2]
            pr.wait("vector", "st", st_done[g % 2])
            pr.op("vector", "tensor_tensor", osb, resid2[:, g, :], ps_c.ap(),
                  op=ALU.mult, sem="ve")
            pr.wait("vector", "ve")
            tve = pr.op("vector", "tensor_sub", osb, osb, ps_d.ap(), sem="ve")
            pr.wait("sync", "ve", tve)
            st_done[g % 2] = pr.dma("sync", out_d[g * 128:(g + 1) * 128,
                                                  th * 512:th * 512 + 512],
                                    osb, "st")

    for b in range(NBLK):
        if stage >= 1:
            proj_block(b)
        if stage >= 2:
            for lc in range(4):
                ssd_chunk(b * 4 + lc)
                if paranoid >= 1:
                    for e in ("tensor", "scalar", "gpsimd", "sync"):
                        pr.wait(e, "ve")
                    pr.wait("vector", "ac")
                    pr.wait("vector", "mm")
                    pr.wait("vector", "x")
                    pr.wait("vector", "gs0")
                    pr.wait("vector", "gs1")
        if stage < 2 and b == 0:
            break
    if "d_spx" in dbg_t:
        pr.wait("sync", "gs0")
        pr.wait("sync", "gs1")
        pr.dma("sync", dbg_t["d_spx"], xsp, "st")
        pr.dump_barrier()
        pr.dma("sync", dbg_t["d_spb"], bsp, "st")
        pr.dump_barrier()
        pr.dma("sync", dbg_t["d_spc"], csp, "st")
        pr.dump_barrier()
    if stage >= 3:
        phase_drain()
        relay()
    if stage >= 4:
        phase_drain()
        pass_b1()
        phase_drain()
        pass_b2()
        phase_drain()
        pass_b3()
    if stage >= 5:
        phase_drain()
        ffn()

    pr.wait("sync", "st")
    pr.wait("sync", "cc")
    pr.wait("gpsimd", "st")
    for sname in ("gs0", "gs1", "gxi", "gbi", "gci", "gar", "gsl", "gxb",
                  "g0", "g1", "g2", "g3", "g4", "g5"):
        pr.wait("sync", sname)

    with nc.Block() as block:
        @block.sync
        def _(e):
            for f in pr.streams["sync"]:
                f(e)

        @block.tensor
        def _(e):
            for f in pr.streams["tensor"]:
                f(e)

        @block.vector
        def _(e):
            for f in pr.streams["vector"]:
                f(e)

        @block.scalar
        def _(e):
            for f in pr.streams["scalar"]:
                f(e)

        @block.gpsimd
        def _(e):
            for f in pr.streams["gpsimd"]:
                f(e)
    for t in reversed(ctx):
        t.__exit__(None, None, None)
    return nc


def _prep_inputs(inputs):
    x = np.asarray(inputs["L0_attn_x"])[0]
    xT_full = np.ascontiguousarray(x.T)
    winT = np.ascontiguousarray(np.asarray(inputs["L0_attn_Win"]).T)
    woutT = np.ascontiguousarray(np.asarray(inputs["L0_attn_Wout"]).T)
    gwT = np.ascontiguousarray(np.asarray(inputs["L0_ffn_gate_w"]).T)
    uwT = np.ascontiguousarray(np.asarray(inputs["L0_ffn_up_w"]).T)
    dwT = np.ascontiguousarray(np.asarray(inputs["L0_ffn_down_w"]).T)
    cwx = np.ascontiguousarray(np.asarray(inputs["L0_attn_conv_w_x"])
                               .reshape(16, 128, KCONV).transpose(1, 0, 2)
                               .reshape(128, 16 * KCONV))
    cwb = np.ascontiguousarray(np.asarray(inputs["L0_attn_conv_w_b"])
                               .reshape(32, 128, KCONV).transpose(1, 0, 2)
                               .reshape(128, 32 * KCONV))
    cwc = np.ascontiguousarray(np.asarray(inputs["L0_attn_conv_w_c"])
                               .reshape(32, 128, KCONV).transpose(1, 0, 2)
                               .reshape(128, 32 * KCONV))
    gs = np.ascontiguousarray(np.asarray(inputs["L0_ffn_gate_scale"])
                              .reshape(DI // 128, 128).T)
    us = np.ascontiguousarray(np.asarray(inputs["L0_ffn_up_scale"])
                              .reshape(DI // 128, 128).T)
    ds = np.ascontiguousarray(np.asarray(inputs["L0_ffn_down_scale"])
                              .reshape(D // 128, 128).T)
    alog = np.ascontiguousarray(np.asarray(inputs["L0_attn_A_log"])
                                .reshape(1, H))
    dtbias = np.ascontiguousarray(np.asarray(inputs["L0_attn_dt_bias"])
                                  .reshape(H, 1))
    in_maps = []
    for c in range(NCORE):
        t0 = c * TC
        xh = np.zeros((D, 3 + TC), np.float32)
        lo = max(0, t0 - 3)
        if t0 > lo:
            xh[:, 3 - (t0 - lo):3] = xT_full[:, lo:t0]
        xh[:, 3:] = xT_full[:, t0:t0 + TC]
        oh = np.zeros((1, NCORE), np.float32)
        oh[0, c] = 1.0
        in_maps.append({
            "xT": np.ascontiguousarray(xh), "winT": winT, "woutT": woutT,
            "alog": alog, "dtbias": dtbias, "cwx": cwx, "cwb": cwb, "cwc": cwc,
            "gwT": gwT, "uwT": uwT, "dwT": dwT, "gs": gs, "us": us, "ds": ds,
            "onehot": oh,
        })
    return in_maps


def kernel(**inputs):
    in_maps = _prep_inputs(inputs)
    nc = build(paranoid=2)
    res = run_bass_kernel_spmd(nc, in_maps, core_ids=list(range(NCORE)))
    out = np.empty((1, T, D), np.float32)
    for c in range(NCORE):
        out[0, c * TC:(c + 1) * TC, :] = res.results[c]["out"].T
    return out
